# version 03 (final)
# baseline (speedup 1.0000x reference)
from contextlib import ExitStack

import numpy as np
import ml_dtypes

import concourse.bass as bass
import concourse.mybir as mybir
from concourse.bass_utils import run_bass_kernel_spmd

F32 = mybir.dt.float32
BF16 = mybir.dt.bfloat16
AF = mybir.ActivationFunctionType
ALU = mybir.AluOpType
AX = mybir.AxisListType
BF = ml_dtypes.bfloat16
f32 = np.float32

T = 2048
D = 1024
DI = 2816
NCH = 16
NK = 8
NI = 22
MAGIC = f32(12582912.0)
NMAGIC = f32(-12582912.0)
GATE_GROUPS = [4, 4, 4, 4, 4, 2]
ENGINES = ("sync", "scalar", "vector", "tensor", "gpsimd")


class Plan:
    def __init__(self):
        self.streams = {e: [] for e in ENGINES}
        self.cnt = {}
        self.bundles = []
        self.cur_bundle = None

    def _lst(self, eng):
        return self.cur_bundle[eng] if self.cur_bundle is not None else self.streams[eng]

    def emit(self, eng, fn):
        self._lst(eng).append(("op", fn))

    def inc(self, eng, fn, sname, k=1):
        self.cnt[sname] = self.cnt.get(sname, 0) + k
        self._lst(eng).append(("inc", fn, sname, k))
        return self.cnt[sname]

    def inc_prev(self, eng, sname, k=1):
        lst = self._lst(eng)
        item = lst[-1]
        assert item[0] == "op", item
        self.cnt[sname] = self.cnt.get(sname, 0) + k
        lst[-1] = ("inc", item[1], sname, k)
        return self.cnt[sname]

    def wait(self, eng, sname, val=None):
        if val is None:
            val = self.cnt.get(sname, 0)
        if val <= 0:
            return
        self._lst(eng).append(("wait", sname, val))

    def now(self, sname):
        return self.cnt.get(sname, 0)

    def start_bundle(self):
        self.cur_bundle = {e: [] for e in ENGINES}
        self.bundles.append(self.cur_bundle)

    def end_bundles(self):
        self.cur_bundle = None

    def drain_bundle(self):
        if self.bundles:
            b = self.bundles.pop(0)
            for e in ENGINES:
                self.streams[e].extend(b[e])

    def drain_all_bundles(self):
        while self.bundles:
            self.drain_bundle()


def build_nc(debug=False, debug2=False):
    nc = bass.Bass("TRN2", target_bir_lowering=False)

    def din(name, shape, dt=F32):
        return nc.dram_tensor(name, shape, dt, kind="ExternalInput").ap()

    x_in = din("x", [T, D])
    wkT = din("wkT", [D, D]); wvT = din("wvT", [D, D]); woT = din("woT", [D, D])
    bkr = din("bkr", [1, D]); bvr = din("bvr", [1, D]); bor = din("bor", [1, D])
    lng = din("lng", [128, NK]); lnb = din("lnb", [128, NK])
    wg_in = din("wg", [DI, D]); wu_in = din("wu", [DI, D]); wd_in = din("wd", [D, DI])
    clipx = din("clipx", [1, 1]); clipd = din("clipd", [1, 1])
    ltt = din("ltt", [128, 128], BF16)
    oneh = din("oneh", [128, NCH * 17], BF16)
    slt = din("slt", [17, 16])
    ones1b_in = din("ones1b", [1, 128], BF16)
    onesf_in = din("onesf", [1, 128])
    ohr_in = din("ohr", [16, NCH * 128], BF16)
    epspos_in = din("epspos", [128, NCH])
    ccmask = din("ccmask", [1, 1])

    out_hbm = nc.dram_tensor("out", [T, D], F32, kind="ExternalOutput").ap()

    spk = "ExternalOutput" if debug else "Internal"
    xn_sp = nc.dram_tensor("xn_sp", [T, D], F32, kind=spk).ap()
    xnb_sp = nc.dram_tensor("xnb_sp", [T, D], BF16, kind=spk).ap()
    ix_sp = nc.dram_tensor("ix_sp", [T, D], BF16, kind=spk).ap()
    nrm_sp = nc.dram_tensor("nrm_sp", [T, D], BF16, kind=spk).ap()
    s_sp = nc.dram_tensor("s_sp", [T, D], F32,
                          kind="ExternalOutput" if (debug or debug2) else "Internal").ap()
    h_sp = nc.dram_tensor("h_sp", [DI, T], BF16, kind=spk).ap()
    iwg_sp = nc.dram_tensor("iwg_sp", [DI, D], BF16, kind=spk).ap()
    iwu_sp = nc.dram_tensor("iwu_sp", [DI, D], BF16, kind=spk).ap()
    iwd_sp = nc.dram_tensor("iwd_sp", [D, DI], BF16,
                            kind="ExternalOutput" if (debug or debug2) else "Internal").ap()
    wotb_sp = nc.dram_tensor("wotb_sp", [D, D], BF16).ap()
    scdbg = nc.dram_tensor("scdbg", [1, 16], F32, kind=spk).ap() if debug else None
    stdbg = nc.dram_tensor("stdbg", [128, 96], F32, kind=spk).ap() if debug else None
    mx_sp = nc.dram_tensor("mx_sp", [1, 128], F32).ap()
    mh_sp = nc.dram_tensor("mh_sp", [1, 128], F32).ap()
    swd_sp = nc.dram_tensor("swd_sp", [1, D], F32,
                            kind="ExternalOutput" if (debug or debug2) else "Internal").ap()
    swdcdbg = (nc.dram_tensor("swdcdbg", [128, 16], F32, kind="ExternalOutput").ap()
               if (debug or debug2) else None)
    swdbdbg = (nc.dram_tensor("swdbdbg", [128, D], F32, kind="ExternalOutput").ap()
               if (debug or debug2) else None)
    ccp_i = nc.dram_tensor("ccp_i", [1, D], F32)
    ccp_o = nc.dram_tensor("ccp_o", [1, D], F32)
    cm1_i = nc.dram_tensor("cm1_i", [1, 16], F32)
    cm1_o = nc.dram_tensor("cm1_o", [1, 16], F32)
    cm2_i = nc.dram_tensor("cm2_i", [1, 16], F32)
    cm2_o = nc.dram_tensor("cm2_o", [1, 16], F32)

    with ExitStack() as es:
        def sb(name, shape, dt=F32):
            return es.enter_context(nc.sbuf_tensor(name, shape, dt))

        R1 = sb("R1", [128, 32768], BF16)
        RIX = sb("RIX", [128, 16384], BF16)
        RW = sb("RW", [128, 16384], BF16)
        R2 = sb("R2", [128, 6144], BF16)
        xr = sb("xr", [128, 4096])
        fsc = sb("fsc", [128, 2816])
        silu_r = sb("silu_r", [128, 2048], BF16)
        qwb = sb("qwb", [128, 2816], BF16)
        hout = sb("hout", [128, 2048], BF16)
        aux = sb("aux", [128, 1024])
        pob = sb("pob", [16, 1024], BF16)
        ohr = sb("ohr_t", [16, NCH * 128], BF16)
        rowa = sb("rowa", [1, 1024])
        rowb = sb("rowb", [1, 128])
        bb = sb("bb", [1, 3 * D], BF16)
        qs2 = sb("qs2", [128, 512], BF16)
        gapb = sb("gapb", [128, 8])
        sumx = sb("sumx", [128, NCH]); sumq = sb("sumq", [128, NCH])
        mcol = sb("mcol", [128, NCH]); vcol = sb("vcol", [128, NCH]); icol = sb("icol", [128, NCH])
        tcol = sb("tcol", [128, NCH])
        sumx2 = sb("sumx2", [128, NCH]); sumq2 = sb("sumq2", [128, NCH])
        mcol2 = sb("mcol2", [128, NCH]); vcol2 = sb("vcol2", [128, NCH]); icol2 = sb("icol2", [128, NCH])
        tcol2 = sb("tcol2", [128, NCH])
        wdmx = sb("wdmx", [128, 2])
        mxc = sb("mxc", [128, NCH + 1])
        mhc = sb("mhc", [128, 45])
        epsc = sb("epsc", [128, NCH])
        lngc = sb("lngc", [128, NK]); lnbc = sb("lnbc", [128, NK])
        lttb = sb("lttb", [128, 128], BF16)
        onehb = sb("onehb", [128, NCH * 17], BF16)
        sltb = sb("sltb", [17, 16])
        ones1 = sb("ones1", [1, 128], BF16)
        onesft = sb("onesft", [1, 128])
        swgc = sb("swgc", [128, NI]); swuc = sb("swuc", [128, NI]); swdc = sb("swdc", [128, NK])
        rsw = sb("rsw", [128, 2])
        colg = sb("colg", [128, NI]); colu = sb("colu", [128, NI])
        sc = sb("sc", [1, 16])
        sccol = sb("sccol", [128, 4])
        gmrow = sb("gmrow", [1, 16]); hmrow2 = sb("hmrow2", [1, 16])
        maskt = sb("maskt", [1, 1])
        clipxt = sb("clipxt", [1, 1]); clipdt = sb("clipdt", [1, 1])

        PA = es.enter_context(nc.psum_tensor("PA", [128, 1024], F32))
        PB = es.enter_context(nc.psum_tensor("PB", [128, 1024], F32))
        PC = es.enter_context(nc.psum_tensor("PC", [128, 1024], F32))
        PS17 = es.enter_context(nc.psum_tensor("PS17", [17, 1024], F32))

        sems = {}
        for nm in ["qld", "qst", "qtp", "qxn", "qx2", "qh", "qcc", "qmx", "qs2r",
                   "c_ln", "c_sq", "c_var", "c_inv", "c_xn", "c_wcast", "c_bof",
                   "c_ev", "c_kn", "c_bnd", "c_lt", "c_mem", "c_s17", "c_rsx",
                   "c_ix", "c_pp", "c_pp2", "c_pbc", "c_pad", "c_ret",
                   "c_l2s", "c_l2c", "c_l2i", "c_nrm", "c_kv", "c_gq", "c_gup",
                   "c_sil", "c_h", "c_hm", "c_wo", "c_sev", "c_rsh", "c_swd",
                   "c_ihf", "c_ihv", "c_dn", "c_dt", "c_dg", "c_cc1", "c_ccp", "c_cc2",
                   "c_beffe", "c_bob", "c_sig", "c_gm", "c_fin", "c_vsy", "c_ssy",
                   "c_vsyb", "c_ssyb"]:
            sems[nm] = es.enter_context(nc.semaphore(nm))

        P = Plan()

        knb_ch = lambda mc: R1[:, mc * D:(mc + 1) * D]
        memb_ch = lambda mc: R1[:, 16384 + mc * D:16384 + (mc + 1) * D]

        def ihf_tile(mi):
            if mi < 16:
                return R1[:, mi * 2048:(mi + 1) * 2048]
            return RIX[:, (mi - 16) * 2048:(mi - 16 + 1) * 2048]

        xnbf_tile = lambda k: R2[:, k * 512:(k + 1) * 512]
        retr_ch = lambda mc: R2[:, 4096 + (mc % 2) * D:4096 + (mc % 2 + 1) * D]
        ixf_tile = lambda k: RIX[:, k * 2048:(k + 1) * 2048]
        bndr_ch = lambda mc: RIX[:, (mc % 4) * D:(mc % 4 + 1) * D]
        vb_sl = lambda mc: RIX[:, 4096 + (mc % 2) * D:4096 + (mc % 2 + 1) * D]
        kb_sl = lambda mc: RIX[:, 6144 + (mc % 2) * D:6144 + (mc % 2 + 1) * D]
        s4_sl = lambda mc: RIX[:, 8192 + (mc % 2) * 256:8192 + (mc % 2 + 1) * 256]
        ri_sl = lambda mc: RIX[:, 8704 + (mc % 2) * 256:8704 + (mc % 2 + 1) * 256]
        q1_sl = lambda mc: RIX[:, 9216 + (mc % 2) * 256:9216 + (mc % 2 + 1) * 256]
        wkb_ch = lambda k: RW[:, k * 1024:(k + 1) * 1024]
        wvb_ch = lambda k: RW[:, 8192 + k * 1024:8192 + (k + 1) * 1024]
        wob_ch = lambda k: RW[:, 8192 + k * 1024:8192 + (k + 1) * 1024]

        def gw_tile(b, mat, k, gc):
            base = b * 8192 + mat * 4096
            return RW[:, base + k * 512:base + k * 512 + gc * 128]

        nrmf_tile = lambda k: RW[:, k * 1024:(k + 1) * 1024]

        def iwdT_tile(kk):
            if kk < 16:
                return RW[:, kk * 1024:(kk + 1) * 1024]
            return R2[:, (kk - 16) * 1024:(kk - 16 + 1) * 1024]

        wst_sl = lambda s: xr[:, s * 1408:s * 1408 + 1408]
        xnr = xr[:, 0:2048]
        xr2 = xr[:, 0:2048]
        sbufs = xr[:, 2048:4096]
        x_sl = lambda c: xr[:, (c % 4) * D:(c % 4 + 1) * D]
        fsc_sl = lambda j: fsc[:, (j % 2) * 1408:(j % 2) * 1408 + 1024]
        fsc_w = lambda s: fsc[:, s * 1408:s * 1408 + 1408]
        scrS1 = qwb[:, 0:1024]
        scrV1 = qwb[:, 1024:2048]
        scrS2 = xr[:, 2048:3072]
        scrV2 = xr[:, 3072:4096]
        xnw_sl = lambda c: fsc[:, (c % 2) * 1408:(c % 2) * 1408 + 1024]
        xnbw_sl = lambda c: silu_r[:, (c % 2) * 1024:(c % 2 + 1) * 1024]
        silu_sl = lambda st: silu_r[:, (st % 2) * 1024:(st % 2 + 1) * 1024]
        qwb_sl = lambda s: qwb[:, s * 1408:s * 1408 + 1408]
        nrmr_sl = lambda c: qwb[:, (c % 2) * 1024:(c % 2 + 1) * 1024]
        hout_sl = lambda st: hout[:, (st % 2) * 1024:(st % 2 + 1) * 1024]
        hin_sl = lambda j: hout[:, (j % 2) * 1024:(j % 2 + 1) * 1024]
        S17 = aux[0:17, :]
        Pp = pob[0:16, :]
        beff = aux[0:1, :]
        prow = rowa[:, :]
        swdrow = rowa[:, :]
        mrow = rowb[:, :]
        mhrow = rowb[:, :]
        shrow = rowb[:, :]
        swdb = aux[:, :]
        bkb = bb[0:1, 0:D]; bvb = bb[0:1, D:2 * D]; bob = bb[0:1, 2 * D:3 * D]
        pl = lambda ch_ap, p: ch_ap[:, p * 256:(p + 1) * 256]

        def _esync(eng, sname):
            if P.cur_bundle is not None:
                sname = sname + "b"
            lst = P._lst(eng)
            i = len(lst) - 1
            while i >= 0 and lst[i][0] == "wait":
                i -= 1
            assert i >= 0
            item = lst[i]
            if item[0] == "op":
                P.cnt[sname] = P.cnt.get(sname, 0) + 1
                lst[i] = ("inc", item[1], sname, 1)
                P.wait(eng, sname, P.cnt[sname])
            else:
                P.wait(eng, item[2], None)

        def vsync():
            _esync("vector", "c_vsy")

        def ssync():
            _esync("scalar", "c_ssy")

        vgap = vsync
        sgap = ssync

        def dma(eng, out, in_, sname, transpose=False):
            return P.inc(eng, lambda e, _o=out, _i=in_, _t=transpose:
                         e.dma_start(out=_o, in_=_i, transpose=_t), sname, 16)

        for dst, src in [(lttb[:], ltt), (onehb[:], oneh), (sltb[:], slt),
                         (ones1[:], ones1b_in), (onesft[:], onesf_in),
                         (epsc[:], epspos_in), (lngc[:], lng), (lnbc[:], lnb),
                         (maskt[:], ccmask), (clipxt[:], clipx), (clipdt[:], clipd),
                         (ohr[:], ohr_in)]:
            dma("sync", dst, src, "qld")
        dma("sync", rowa[:, :], bkr, "qld")
        qld_consts = P.now("qld")
        xld_m = {}
        for i in range(2):
            dma("sync", xr[:, (2 * i % 4) * D:(2 * i % 4 + 2) * D].rearrange("p (c d) -> p c d", d=D),
                x_in.rearrange("(c p) d -> p c d", p=128)[:, 2 * i:2 * i + 2, :], "qld")
            xld_m[i] = P.now("qld")
        P.wait("vector", "qld", qld_consts)
        P.emit("vector", lambda v: v.memset(sc[:], 0.0))
        P.emit("vector", lambda v: v.memset(mxc[:], 0.0))
        P.emit("vector", lambda v: v.memset(mhc[:], 0.0))
        bst1 = P.inc("vector", lambda v: v.tensor_copy(bkb, rowa[:, :]), "c_sig")
        P.wait("sync", "c_sig", bst1)
        dma("sync", rowa[:, :], bvr, "qld")
        qbv = P.now("qld")
        P.wait("vector", "qld", qbv)
        bst2 = P.inc("vector", lambda v: v.tensor_copy(bvb, rowa[:, :]), "c_sig")
        P.wait("sync", "c_sig", bst2)
        dma("sync", rowa[:, :], bor, "qld")
        qbo = P.now("qld")
        memv = bst2
        P.wait("scalar", "qld", qld_consts)
        P.wait("scalar", "c_sig", memv)
        P.emit("scalar", lambda s: s.activation(sc[0:1, 0:1], clipxt[:], AF.Sigmoid))
        P.inc("scalar", lambda s: s.activation(sc[0:1, 1:2], clipdt[:], AF.Sigmoid), "c_sig")

        for i in range(2, 8):
            P.wait("sync", "c_xn", 2 * i - 2)
            dma("sync", xr[:, (2 * i % 4) * D:(2 * i % 4 + 2) * D].rearrange("p (c d) -> p c d", d=D),
                x_in.rearrange("(c p) d -> p c d", p=128)[:, 2 * i:2 * i + 2, :], "qld")
            xld_m[i] = P.now("qld")
        xnb_spill_m = {}
        xn_spill_m = {}
        for c in range(NCH):
            xc = x_sl(c)
            xnc = xnw_sl(c)
            xnbc = xnbw_sl(c)
            P.wait("vector", "qld", xld_m[c // 2])
            P.inc("vector", lambda v, a=xc, cc=c:
                  v.tensor_scalar(scrV1, a, 1.0, 0.0, ALU.mult, ALU.add,
                                  accum_out=sumx[:, cc:cc + 1]), "c_ln")
            P.wait("scalar", "qld", xld_m[c // 2])
            sqm = P.inc("scalar", lambda s, a=xc, cc=c:
                        s.activation(scrS1, a, AF.Square, accum_out=sumq[:, cc:cc + 1]), "c_sq")
            P.wait("vector", "c_sq", sqm)
            vgap()
            P.emit("vector", lambda v, cc=c:
                   v.tensor_scalar(mcol[:, cc:cc + 1], sumx[:, cc:cc + 1], float(1.0 / D), None, ALU.mult))
            P.emit("vector", lambda v, cc=c:
                   v.tensor_scalar(tcol[:, cc:cc + 1], sumq[:, cc:cc + 1], float(1.0 / D), None, ALU.mult))
            P.emit("vector", lambda v, cc=c:
                   v.tensor_mul(vcol[:, cc:cc + 1], mcol[:, cc:cc + 1], mcol[:, cc:cc + 1]))
            vgap()
            varm = P.inc("vector", lambda v, cc=c:
                         v.tensor_sub(vcol[:, cc:cc + 1], tcol[:, cc:cc + 1], vcol[:, cc:cc + 1]),
                         "c_var")
            P.wait("scalar", "c_var", varm)
            P.emit("scalar", lambda s, cc=c:
                   s.activation(icol[:, cc:cc + 1], vcol[:, cc:cc + 1], AF.Ln, bias=f32(1e-5)))
            ssync()
            invm = P.inc("scalar", lambda s, cc=c:
                         s.activation(icol[:, cc:cc + 1], icol[:, cc:cc + 1], AF.Exp,
                                      scale=f32(-0.5)), "c_inv")
            if c >= 2:
                P.wait("vector", "qst", xnb_spill_m[c - 2])
            P.wait("vector", "c_inv", invm)
            P.emit("vector", lambda v, a=xnc, b=xc, m=mcol[:, c:c + 1], i=icol[:, c:c + 1]:
                   v.tensor_scalar(a, b, m, i, ALU.subtract, ALU.mult))
            vgap()
            P.emit("vector", lambda v, a=xnbc, b=xnc: v.tensor_copy(a, b))
            P.inc("vector", lambda v, b=xnc, cc=c:
                  v.tensor_reduce(mxc[:, cc:cc + 1], b, AX.X, ALU.max,
                                  apply_absolute_value=True), "c_xn")
            P.wait("scalar", "c_xn")
            xn_spill_m[c] = dma("scalar", xn_sp[c * 128:(c + 1) * 128, :], xnc, "qst")
            xnb_spill_m[c] = dma("scalar", xnb_sp[c * 128:(c + 1) * 128, :], xnbc, "qst")
        vgap(); vgap()
        P.inc("vector", lambda v: v.tensor_reduce(mxc[:, 16:17], mxc[:, 0:16], AX.X, ALU.max), "c_xn")
        P.wait("scalar", "c_xn")
        dma("scalar", mx_sp[0:1, :].rearrange("a b -> b a"), mxc[:, 16:17], "qmx")
        P.wait("sync", "qmx")
        dma("sync", mrow[:], mx_sp, "qmx")
        P.wait("vector", "qmx")
        P.inc("vector", lambda v: v.tensor_reduce(sc[0:1, 2:3], mrow[:], AX.X, ALU.max), "c_gm")
        P.wait("scalar", "c_gm")
        dma("scalar", cm1_i.ap(), sc[0:1, 0:16], "qmx")
        P.wait("gpsimd", "qmx")
        P.inc("gpsimd", lambda g: g.collective_compute(
            "AllReduce", ALU.max, replica_groups=[list(range(8))],
            ins=[cm1_i.ap().opt()], outs=[cm1_o.ap().opt()]), "c_cc1")
        P.wait("gpsimd", "c_cc1")
        dma("gpsimd", gmrow[:], cm1_o.ap(), "qcc")
        qcc_gm = P.now("qcc")
        P.wait("vector", "qcc", qcc_gm)
        P.emit("vector", lambda v: v.tensor_mul(sc[0:1, 8:9], sc[0:1, 0:1], gmrow[0:1, 2:3]))
        vgap()
        tsm = P.inc("vector", lambda v: v.tensor_scalar(sc[0:1, 4:5], sc[0:1, 8:9],
                                                        float(1.0 / 127.0), None, ALU.mult), "c_rsx")
        P.wait("scalar", "c_rsx", tsm)
        P.emit("scalar", lambda s: s.activation(sc[0:1, 5:6], sc[0:1, 8:9], AF.Ln,
                                                scale=f32(1.0 / 127.0)))
        ssync()
        P.inc("scalar", lambda s: s.activation(sc[0:1, 5:6], sc[0:1, 5:6], AF.Exp,
                                               scale=f32(-1.0)), "c_rsx")

        P.wait("sync", "c_xn", 16)
        wo_spill = {}
        for wi, wsrc in enumerate([wkT, wvT, woT]):
            for q in range(4):
                idx = wi * 4 + q
                if idx >= 1:
                    P.wait("sync", "c_wcast", idx)
                    if wi == 2 and q >= 1:
                        P.wait("sync", "c_bof", P.now("c_bof"))
                dma("sync", xr[:, 0:2 * D].rearrange("p (c d) -> p c d", d=D),
                    wsrc.rearrange("(c p) d -> p c d", p=128)[:, 2 * q:2 * q + 2, :], "qld")
                ldm = P.now("qld")
                if wi == 2:
                    P.wait("tensor", "qld", ldm)
                    for j in range(2):
                        k = 2 * q + j
                        for n in range(2):
                            P.emit("tensor", lambda t, kk=k, jj=j, nn=n:
                                   t.matmul(PS17[0:1, nn * 512:(nn + 1) * 512],
                                            lnbc[:, kk:kk + 1],
                                            xr[:, jj * D + nn * 512:jj * D + (nn + 1) * 512],
                                            start=(kk == 0), stop=(kk == 7)))
                    P.inc_prev("tensor", "c_bof")
                P.wait("vector", "qld", ldm)
                if wi == 2:
                    P.wait("vector", "c_bof", P.now("c_bof"))
                    if q >= 1:
                        P.wait("vector", "qst", wo_spill[2 * (q - 1) + 1])
                for j in range(2):
                    k = 2 * q + j
                    src_ch = xr[:, j * D:(j + 1) * D]
                    if wi == 0:
                        P.emit("vector", lambda v, a=wkb_ch(k), b=src_ch: v.tensor_copy(a, b))
                    elif wi == 1:
                        P.emit("vector", lambda v, a=wvb_ch(k), b=src_ch: v.tensor_copy(a, b))
                    else:
                        P.emit("vector", lambda v, a=qwb[:, (j % 2) * 1024:(j % 2 + 1) * 1024],
                               b=src_ch, g2=lngc[:, k:k + 1]:
                               v.tensor_scalar(a, b, g2, None, ALU.mult))
                wcm = P.inc_prev("vector", "c_wcast")
                if wi == 2:
                    P.wait("scalar", "c_wcast", wcm)
                    for j in range(2):
                        k = 2 * q + j
                        wo_spill[k] = dma("scalar", wotb_sp[k * 128:(k + 1) * 128, :],
                                          qwb[:, (j % 2) * 1024:(j % 2 + 1) * 1024], "qst")
        P.wait("scalar", "c_bof")
        P.inc("scalar", lambda s: s.copy(beff, PS17[0:1, :]), "c_beffe")
        P.wait("vector", "c_beffe")
        P.wait("vector", "qld", qbo)
        P.emit("vector", lambda v: v.tensor_add(beff, beff, rowa[:, :]))
        vgap()
        P.inc("vector", lambda v: v.tensor_copy(bob, beff), "c_bob")

        SPV = {0: iwg_sp, 1: iwu_sp}
        SWC = {0: swgc, 1: swuc}
        wq_jobs = ([("gu", 0, c) for c in range(NI)] + [("gu", 1, c) for c in range(NI)]
                   + [("wd", 2, c) for c in range(NK)])
        wq_state = {"i": 0}

        def emit_wq_job():
            i = wq_state["i"]
            if i >= len(wq_jobs):
                return False
            wq_state["i"] += 1
            kind, mat, cch = wq_jobs[i]
            if i == 0:
                P.wait("sync", "c_wcast", 12)
                P.wait("sync", "qst", P.now("qst"))
                P.wait("scalar", "qst", xn_spill_m[NCH - 1])
                P.wait("vector", "qst", P.now("qst"))
            else:
                P.wait("sync", "c_gq", P.now("c_gq"))
            if kind == "wd":
                dma("sync", wst_sl(0), wd_in[cch * 128:(cch + 1) * 128, 0:1408], "qld")
                dma("sync", wst_sl(1), wd_in[cch * 128:(cch + 1) * 128, 1408:2816], "qld")
                ldm = P.now("qld")
                P.wait("vector", "qld", ldm)
                P.emit("vector", lambda v: v.tensor_reduce(wdmx[:, 0:1], wst_sl(0), AX.X,
                                                           ALU.max, apply_absolute_value=True))
                P.emit("vector", lambda v: v.tensor_reduce(wdmx[:, 1:2], wst_sl(1), AX.X,
                                                           ALU.max, apply_absolute_value=True))
                vsync()
                P.emit("vector", lambda v, c2=swdc[:, cch:cch + 1]:
                       v.tensor_max(c2, wdmx[:, 0:1], wdmx[:, 1:2]))
                vgap()
                P.emit("vector", lambda v, c2=swdc[:, cch:cch + 1]:
                       v.tensor_scalar(c2, c2, float(1.0 / 127.0), None, ALU.mult))
                vm = P.inc_prev("vector", "c_gq")
                P.wait("scalar", "c_gq", vm)
                P.emit("scalar", lambda s, c2=swdc[:, cch:cch + 1]:
                       s.activation(rsw[:, 0:1], c2, AF.Ln))
                ssync()
                P.emit("scalar", lambda s: s.activation(rsw[:, 0:1], rsw[:, 0:1], AF.Exp,
                                                        scale=f32(-1.0)))
                ssync()
                P.emit("scalar", lambda s: s.activation(fsc_w(0), wst_sl(0), AF.Identity,
                                                        scale=rsw[:, 0:1], bias=MAGIC))
                P.emit("scalar", lambda s: s.activation(fsc_w(1), wst_sl(1), AF.Identity,
                                                        scale=rsw[:, 0:1], bias=MAGIC))
                sm = P.inc_prev("scalar", "c_gq")
                P.wait("vector", "c_gq", sm)
                P.emit("vector", lambda v: v.tensor_scalar(qwb_sl(0), fsc_w(0), -12582912.0, None, ALU.add))
                P.emit("vector", lambda v: v.tensor_scalar(qwb_sl(1), fsc_w(1), -12582912.0, None, ALU.add))
                vm2 = P.inc_prev("vector", "c_gq")
                P.wait("scalar", "c_gq", vm2)
                dma("scalar", iwd_sp[cch * 128:(cch + 1) * 128, :], qwb[:, 0:2816], "qst")
            else:
                src = {0: wg_in, 1: wu_in}[mat]
                sw = SWC[mat]
                slot = i % 2
                wsl = xr[:, slot * 1408:slot * 1408 + D]
                fsl = fsc[:, slot * 1408:slot * 1408 + D]
                qsl = qwb[:, slot * 1408:slot * 1408 + D]
                rs = rsw[:, slot:slot + 1]
                dma("sync", wsl, src[cch * 128:(cch + 1) * 128, :], "qld")
                ldm = P.now("qld")
                P.wait("vector", "qld", ldm)
                P.emit("vector", lambda v, a=wsl, c2=sw[:, cch:cch + 1]:
                       v.tensor_reduce(c2, a, AX.X, ALU.max, apply_absolute_value=True))
                vgap()
                P.emit("vector", lambda v, c2=sw[:, cch:cch + 1]:
                       v.tensor_scalar(c2, c2, float(1.0 / 127.0), None, ALU.mult))
                vm = P.inc_prev("vector", "c_gq")
                P.wait("scalar", "c_gq", vm)
                P.emit("scalar", lambda s, c2=sw[:, cch:cch + 1], r=rs:
                       s.activation(r, c2, AF.Ln))
                ssync()
                P.emit("scalar", lambda s, r=rs: s.activation(r, r, AF.Exp, scale=f32(-1.0)))
                ssync()
                P.emit("scalar", lambda s, a=fsl, b=wsl, r=rs:
                       s.activation(a, b, AF.Identity, scale=r, bias=MAGIC))
                sm = P.inc_prev("scalar", "c_gq")
                P.wait("vector", "c_gq", sm)
                P.emit("vector", lambda v, a=qsl, b=fsl: v.tensor_scalar(a, b, -12582912.0, None, ALU.add))
                vm2 = P.inc_prev("vector", "c_gq")
                P.wait("scalar", "c_gq", vm2)
                dma("scalar", SPV[mat][cch * 128:(cch + 1) * 128, :], qsl, "qst")
            return True

        QS_BOUND = [
            (0, [(0, 0, 1), (1, 1, -1), (2, 2, -1), (3, 3, -1)]),
            (1, [(0, 1, 1), (1, 0, 1), (2, 3, 1), (3, 2, -1)]),
            (2, [(0, 2, 1), (2, 0, 1), (3, 1, 1), (1, 3, -1)]),
            (3, [(0, 3, 1), (1, 2, 1), (3, 0, 1), (2, 1, -1)]),
        ]
        for qu in range(4):
            P.wait("sync", "qst", xnb_spill_m[qu * 4 + 3])
            if qu >= 1:
                P.wait("sync", "c_kv", 4 * qu)
            for k in range(NK):
                dma("sync", xnbf_tile(k),
                    xnb_sp[qu * 512:(qu + 1) * 512, k * 128:(k + 1) * 128], "qtp", transpose=True)
            qtp_q = P.now("qtp")
            for m in range(4):
                mc = qu * 4 + m
                P.wait("tensor", "qtp", qtp_q)
                if mc == 0:
                    P.wait("tensor", "c_wcast", 8)
                    P.wait("tensor", "c_bob")
                if mc >= 1:
                    P.wait("tensor", "c_ev", 2 * mc)
                for dst_w, wch, brow in [(PA, wkb_ch, bkb), (PB, wvb_ch, bvb)]:
                    for n in range(2):
                        for k in range(NK):
                            P.emit("tensor", lambda t, _d=dst_w, kk=k, nn=n, mm=m, _w=wch:
                                   t.matmul(_d[:, nn * 512:(nn + 1) * 512],
                                            xnbf_tile(kk)[:, mm * 128:(mm + 1) * 128],
                                            _w(kk)[:, nn * 512:(nn + 1) * 512],
                                            start=(kk == 0), stop=False))
                        P.emit("tensor", lambda t, _d=dst_w, nn=n, _b=brow:
                               t.matmul(_d[:, nn * 512:(nn + 1) * 512], ones1[:],
                                        _b[0:1, nn * 512:(nn + 1) * 512], start=False, stop=True))
                kvm = P.inc_prev("tensor", "c_kv")
                P.wait("scalar", "c_kv", kvm)
                if mc >= 2:
                    P.wait("scalar", "c_bnd", mc - 1)
                ks = kb_sl(mc)
                vs = vb_sl(mc)
                P.emit("scalar", lambda s, a=ks: s.copy(a, PA[:, :]))
                evm = P.inc("scalar", lambda s, a=vs: s.copy(a, PB[:, :]), "c_ev", 2)
                P.wait("vector", "c_ev", evm)
                s4 = s4_sl(mc); ri = ri_sl(mc)
                qsc = silu_r
                for p in range(4):
                    P.emit("vector", lambda v, a=qsc[:, p * 256:(p + 1) * 256], kp=pl(ks, p):
                           v.tensor_mul(a, kp, kp))
                vsync()
                P.emit("vector", lambda v, a=qsc[:, 0:256], b=qsc[:, 256:512]:
                       v.tensor_add(a, a, b))
                P.emit("vector", lambda v, a=qsc[:, 512:768], b=qsc[:, 768:1024]:
                       v.tensor_add(a, a, b))
                vsync()
                P.emit("vector", lambda v, a=s4, b=qsc[:, 0:256], c2=qsc[:, 512:768]:
                       v.tensor_add(a, b, c2))
                s4m = P.inc_prev("vector", "c_kn")
                P.wait("scalar", "c_kn", s4m)
                P.emit("scalar", lambda s, a=ri, b=s4: s.activation(a, b, AF.Ln, bias=f32(1e-24)))
                ssync()
                rim = P.inc("scalar", lambda s, a=ri: s.activation(a, a, AF.Exp, scale=f32(-0.5)), "c_kn")
                P.wait("vector", "c_kn", rim)
                for p in range(4):
                    P.emit("vector", lambda v, o=pl(knb_ch(mc), p), kp=pl(ks, p), r=ri:
                           v.tensor_mul(o, kp, r))
                P.inc_prev("vector", "c_kn")
                bnd = bndr_ch(mc)
                if mc >= 4:
                    P.wait("vector", "c_lt", mc - 3)
                kn = knb_ch(mc)

                def emit_qmul(dst, q1t, q2t, table, qsc2):
                    for pair in ((0, 1), (2, 3)):
                        eA = table[pair[0]]; eB = table[pair[1]]
                        for w, ee in ((0, eA), (1, eB)):
                            for ti in range(4):
                                ap1, ap2, _sg = ee[1][ti]
                                sl2 = qsc2[:, (w * 4 + ti) * 256:(w * 4 + ti + 1) * 256]
                                P.emit("vector", lambda v, a=sl2, b=pl(q1t, ap1), c2=pl(q2t, ap2):
                                       v.tensor_mul(a, b, c2))
                        vsync()
                        for w, ee in ((0, eA), (1, eB)):
                            sgs = [t[2] for t in ee[1]]
                            b0 = qsc2[:, (w * 4) * 256:(w * 4 + 1) * 256]
                            b1 = qsc2[:, (w * 4 + 1) * 256:(w * 4 + 2) * 256]
                            b2 = qsc2[:, (w * 4 + 2) * 256:(w * 4 + 3) * 256]
                            b3 = qsc2[:, (w * 4 + 3) * 256:(w * 4 + 4) * 256]
                            op1 = ALU.add if sgs[1] > 0 else ALU.subtract
                            op2 = ALU.add if sgs[2] * sgs[3] > 0 else ALU.subtract
                            P.emit("vector", lambda v, a=b0, b=b1, _o=op1:
                                   v.tensor_tensor(a, a, b, _o))
                            P.emit("vector", lambda v, a=b2, b=b3, _o=op2:
                                   v.tensor_tensor(a, a, b, _o))
                        vsync()
                        for w, ee in ((0, eA), (1, eB)):
                            sgs = [t[2] for t in ee[1]]
                            o_ = pl(dst, ee[0])
                            b0 = qsc2[:, (w * 4) * 256:(w * 4 + 1) * 256]
                            b2 = qsc2[:, (w * 4 + 2) * 256:(w * 4 + 3) * 256]
                            op3 = ALU.add if sgs[2] > 0 else ALU.subtract
                            P.emit("vector", lambda v, a=o_, b=b0, c2=b2, _o=op3:
                                   v.tensor_tensor(a, b, c2, _o))

                emit_qmul(bnd, vs, kn, QS_BOUND, silu_r)
                bndm = P.inc_prev("vector", "c_bnd")
                P.wait("tensor", "c_bnd", bndm)
                if mc >= 1:
                    P.wait("tensor", "c_mem", mc)
                for n in range(2):
                    P.emit("tensor", lambda t, nn=n, b=bnd:
                           t.matmul(PC[:, nn * 512:(nn + 1) * 512], lttb[:],
                                    b[:, nn * 512:(nn + 1) * 512], start=True, stop=True))
                for n in range(2):
                    P.emit("tensor", lambda t, nn=n, b=bnd, mm=mc:
                           t.matmul(PS17[:, nn * 512:(nn + 1) * 512],
                                    onehb[:, mm * 17:(mm + 1) * 17],
                                    b[:, nn * 512:(nn + 1) * 512],
                                    start=(mm == 0), stop=(mm == NCH - 1)))
                ltm = P.inc_prev("tensor", "c_lt")
                P.wait("scalar", "c_lt", ltm)
                P.inc("scalar", lambda s, o=memb_ch(mc): s.copy(o, PC[:, :]), "c_mem")
                for _ in range(2):
                    emit_wq_job()
        while emit_wq_job():
            pass
        qst_after_wq = P.now("qst")

        P.wait("scalar", "c_lt")
        s17m = P.inc("scalar", lambda s: s.copy(S17, PS17[:, :]), "c_s17")
        P.wait("scalar", "c_s17", s17m)
        dma("scalar", ccp_i.ap(), S17[0:1, :], "qmx")
        P.wait("gpsimd", "qmx")
        P.inc("gpsimd", lambda g: g.collective_compute(
            "AllReduce", ALU.add, replica_groups=[[0, 1], [2, 3], [4, 5], [6, 7]],
            ins=[ccp_i.ap().opt()], outs=[ccp_o.ap().opt()]), "c_ccp")
        P.wait("gpsimd", "c_ccp")
        dma("gpsimd", prow, ccp_o.ap(), "qcc")
        qcc_pr = P.now("qcc")

        P.wait("tensor", "c_rsx")
        P.wait("tensor", "c_mem")
        P.inc("tensor", lambda t: t.matmul(PC[:, 0:2], onesft[:], sc[0:1, 4:6],
                                           start=True, stop=True), "c_pbc")
        P.wait("vector", "c_pbc", P.now("c_pbc"))
        sccol_m = P.inc("vector", lambda v: v.tensor_copy(sccol[:, 0:2], PC[:, 0:2]), "c_pbc")
        ix_done = {}
        ix_spill_m = {}
        for c in range(NCH):
            if c == 0:
                P.wait("sync", "c_gq", P.now("c_gq"))
                P.wait("sync", "qst", qst_after_wq)
                P.wait("vector", "qst", qst_after_wq)
            if c >= 2:
                P.wait("sync", "c_ix", ix_done[c - 2])
            dma("sync", xnr[:, (c % 2) * D:(c % 2 + 1) * D], xn_sp[c * 128:(c + 1) * 128, :], "qxn")
            qm = P.now("qxn")
            fsl = fsc_sl(c)
            P.wait("scalar", "qxn", qm)
            if c == 0:
                P.wait("scalar", "c_pbc", sccol_m)
                P.wait("scalar", "c_gq", P.now("c_gq"))
            P.emit("scalar", lambda s, a=fsl, b=xnr[:, (c % 2) * D:(c % 2 + 1) * D]:
                   s.activation(a, b, AF.Identity, scale=sccol[:, 1:2], bias=MAGIC))
            smv = P.inc_prev("scalar", "c_ix")
            P.wait("vector", "c_ix", smv)
            ixsl = qwb[:, (c % 2) * 1024:(c % 2 + 1) * 1024]
            if c >= 2:
                P.wait("vector", "qst", ix_spill_m[c - 2])
            P.emit("vector", lambda v, a=fsl: v.tensor_scalar(a, a, -12582912.0, 127.0, ALU.add, ALU.min))
            vgap()
            P.emit("vector", lambda v, o=ixsl, a=fsl: v.tensor_scalar(o, a, -128.0, None, ALU.max))
            ix_done[c] = P.inc_prev("vector", "c_ix")
            P.wait("scalar", "c_ix", ix_done[c])
            ix_spill_m[c] = dma("scalar", ix_sp[c * 128:(c + 1) * 128, :], ixsl, "qst")
        P.wait("sync", "qst", ix_spill_m[NCH - 1])
        P.wait("sync", "c_lt", 16)
        for k in range(NK):
            dma("sync", ixf_tile(k), ix_sp[0:T, k * 128:(k + 1) * 128], "qtp", transpose=True)
        qtp_ix = P.now("qtp")

        QR_RETR = [
            (0, [(0, 0, 1), (1, 1, 1), (2, 2, 1), (3, 3, 1)]),
            (1, [(1, 0, 1), (0, 1, -1), (2, 3, -1), (3, 2, 1)]),
            (2, [(2, 0, 1), (0, 2, -1), (1, 3, 1), (3, 1, -1)]),
            (3, [(3, 0, 1), (0, 3, -1), (2, 1, 1), (1, 2, -1)]),
        ]
        nrm_spill_m = {}
        P.start_bundle()
        P.wait("vector", "qcc", qcc_pr)
        P.emit("vector", lambda v: v.tensor_sub(prow, prow, S17[0:1, :]))
        vgap()
        P.emit("vector", lambda v: v.tensor_scalar(prow, prow, maskt[0:1, 0:1], None, ALU.mult))
        vgap()
        ppm = P.inc("vector", lambda v: v.tensor_copy(S17[0:1, :], prow), "c_pp")
        P.wait("tensor", "c_pp", ppm)
        for n in range(2):
            P.emit("tensor", lambda t, nn=n:
                   t.matmul(PS17[0:16, nn * 512:(nn + 1) * 512], sltb[:],
                            S17[:, nn * 512:(nn + 1) * 512], start=True, stop=True))
        sltm = P.inc_prev("tensor", "c_pp")
        P.wait("scalar", "c_pp", sltm)
        pp2 = P.inc("scalar", lambda s: s.copy(Pp, PS17[0:16, :]), "c_pp2")
        for mc in range(NCH):
            P.start_bundle()
            P.wait("tensor", "c_pp2", pp2)
            if mc >= 1:
                P.wait("tensor", "c_pad", mc)
            for n in range(2):
                P.emit("tensor", lambda t, nn=n, mm=mc:
                       t.matmul(PC[:, nn * 512:(nn + 1) * 512],
                                ohr[:, mm * 128:(mm + 1) * 128],
                                Pp[:, nn * 512:(nn + 1) * 512], start=True, stop=True))
            pbm = P.inc_prev("tensor", "c_pbc")
            P.wait("vector", "c_pbc", pbm)
            P.inc("vector", lambda v, o=memb_ch(mc): v.tensor_add(o, o, PC[:, :]), "c_pad")
            if mc >= 2:
                P.wait("vector", "c_nrm", mc - 1)
            mm_ = memb_ch(mc)
            kn_ = knb_ch(mc)
            rt = retr_ch(mc)
            vsync()
            emit_qmul(rt, mm_, kn_, QR_RETR, R2[:, 0:2048])
            P.inc_prev("vector", "c_ret")
            P.emit("vector", lambda v, a=rt, cc=mc:
                   v.tensor_scalar(scrV2, a, 1.0, 0.0, ALU.mult, ALU.add,
                                   accum_out=sumx2[:, cc:cc + 1]))
            l2v = P.inc_prev("vector", "c_l2s")
            P.wait("scalar", "c_l2s", l2v)
            P.inc("scalar", lambda s, a=rt, cc=mc:
                  s.activation(scrS2, a, AF.Square, accum_out=sumq2[:, cc:cc + 1]), "c_l2s")
            P.wait("vector", "c_l2s", P.now("c_l2s"))
            vgap()
            P.emit("vector", lambda v, cc=mc:
                   v.tensor_scalar(mcol2[:, cc:cc + 1], sumx2[:, cc:cc + 1],
                                   float(1.0 / D), None, ALU.mult))
            P.emit("vector", lambda v, cc=mc:
                   v.tensor_scalar(tcol2[:, cc:cc + 1], sumq2[:, cc:cc + 1],
                                   float(1.0 / D), None, ALU.mult))
            P.emit("vector", lambda v, cc=mc:
                   v.tensor_mul(vcol2[:, cc:cc + 1], mcol2[:, cc:cc + 1], mcol2[:, cc:cc + 1]))
            vgap()
            l2c = P.inc("vector", lambda v, cc=mc:
                        v.tensor_sub(vcol2[:, cc:cc + 1], tcol2[:, cc:cc + 1],
                                     vcol2[:, cc:cc + 1]), "c_l2c")
            P.wait("scalar", "c_l2c", l2c)
            P.emit("scalar", lambda s, cc=mc:
                   s.activation(icol2[:, cc:cc + 1], vcol2[:, cc:cc + 1], AF.Ln,
                                bias=epsc[:, cc:cc + 1]))
            ssync()
            l2i = P.inc("scalar", lambda s, cc=mc:
                        s.activation(icol2[:, cc:cc + 1], icol2[:, cc:cc + 1],
                                     AF.Exp, scale=f32(-0.5)), "c_l2i")
            P.wait("vector", "c_l2i", l2i)
            nsl = nrmr_sl(mc)
            if mc < 2:
                P.wait("vector", "qst", ix_spill_m[NCH - 1])
            if mc >= 2:
                P.wait("vector", "qst", nrm_spill_m[mc - 2])
            P.inc("vector", lambda v, o=nsl, a=rt, m=mcol2[:, mc:mc + 1],
                  i=icol2[:, mc:mc + 1]:
                  v.tensor_scalar(o, a, m, i, ALU.subtract, ALU.mult), "c_nrm")
            P.wait("scalar", "c_nrm", P.now("c_nrm"))
            nrm_spill_m[mc] = dma("scalar", nrm_sp[mc * 128:(mc + 1) * 128, :], nsl, "qst")
        P.end_bundles()
        nrm_spill_all = P.now("qst")

        P.wait("vector", "c_gq")
        P.wait("vector", "c_pbc", sccol_m)
        P.emit("vector", lambda v: v.tensor_scalar(colg[:], swgc[:], sccol[:, 0:1], None, ALU.mult))
        cgm = P.inc("vector", lambda v: v.tensor_scalar(colu[:], swuc[:], sccol[:, 0:1],
                                                        None, ALU.mult), "c_gup")
        step = 0
        hsp_m = {}
        base = 0
        for gi, gc in enumerate(GATE_GROUPS):
            b = gi % 2
            if gi >= 2:
                prev_steps = 2 * sum(GATE_GROUPS[:gi - 1])
                P.wait("sync", "c_h", prev_steps)
            if gi == 0:
                P.wait("sync", "qst", qst_after_wq)
            for k in range(NK):
                dma("sync", gw_tile(b, 0, k, gc),
                    iwg_sp[base * 128:(base + gc) * 128, k * 128:(k + 1) * 128], "qtp", transpose=True)
                dma("sync", gw_tile(b, 1, k, gc),
                    iwu_sp[base * 128:(base + gc) * 128, k * 128:(k + 1) * 128], "qtp", transpose=True)
            qtp_g = P.now("qtp")
            for mi_loc in range(gc):
                mi = base + mi_loc
                for th in range(2):
                    P.wait("tensor", "qtp", qtp_g)
                    if step == 0:
                        P.wait("tensor", "qtp", qtp_ix)
                        P.wait("tensor", "c_gup", cgm)
                    if step >= 1:
                        P.wait("tensor", "c_h", step)
                    for mat, dst in [(0, PA), (1, PB)]:
                        for n in range(2):
                            for k in range(NK):
                                P.emit("tensor", lambda t, _d=dst, _m=mat, kk=k, nn=n,
                                       ml=mi_loc, tt=th, bb=b, g2=gc:
                                       t.matmul(_d[:, nn * 512:(nn + 1) * 512],
                                                gw_tile(bb, _m, kk, g2)[:, ml * 128:(ml + 1) * 128],
                                                ixf_tile(kk)[:, tt * 1024 + nn * 512:
                                                              tt * 1024 + (nn + 1) * 512],
                                                start=(kk == 0), stop=(kk == NK - 1)))
                    gupm = P.inc_prev("tensor", "c_gup")
                    P.wait("scalar", "c_gup", gupm)
                    ssl = silu_sl(step)
                    silm = P.inc("scalar", lambda s, o=ssl, c2=colg[:, mi:mi + 1]:
                                 s.activation(o, PA[:, :], AF.Silu, scale=c2), "c_sil")
                    P.wait("vector", "c_sil", silm)
                    hsl = hout_sl(step)
                    if step >= 2:
                        P.wait("vector", "qh", hsp_m[step - 2])
                    P.emit("vector", lambda v, o=hsl, c2=colu[:, mi:mi + 1], s2=ssl:
                           v.scalar_tensor_tensor(o, PB[:, :], c2, s2, ALU.mult, ALU.mult))
                    vgap()
                    P.emit("vector", lambda v, a=hsl, st2=step:
                           v.tensor_reduce(mhc[:, st2:st2 + 1], a, AX.X, ALU.max,
                                           apply_absolute_value=True))
                    hm = P.inc_prev("vector", "c_h")
                    P.wait("scalar", "c_h", hm)
                    hsp_m[step] = dma("scalar", h_sp[mi * 128:(mi + 1) * 128,
                                                     th * 1024:(th + 1) * 1024], hsl, "qh")
                    step += 1
                    P.drain_bundle()
            base += gc
        P.drain_all_bundles()

        P.wait("vector", "c_h")
        vgap(); vgap()
        P.inc("vector", lambda v: v.tensor_reduce(mhc[:, 44:45], mhc[:, 0:44], AX.X, ALU.max), "c_hm")
        P.wait("scalar", "c_hm")
        dma("scalar", mh_sp[0:1, :].rearrange("a b -> b a"), mhc[:, 44:45], "qmx")
        P.wait("sync", "qmx")
        dma("sync", mhrow, mh_sp, "qmx")
        P.wait("vector", "qmx")
        P.inc("vector", lambda v: v.tensor_reduce(sc[0:1, 3:4], mhrow, AX.X, ALU.max), "c_hm")
        P.wait("scalar", "c_hm")
        dma("scalar", cm2_i.ap(), sc[0:1, 0:16], "qmx")
        P.wait("gpsimd", "qmx")
        P.inc("gpsimd", lambda g: g.collective_compute(
            "AllReduce", ALU.max, replica_groups=[list(range(8))],
            ins=[cm2_i.ap().opt()], outs=[cm2_o.ap().opt()]), "c_cc2")
        P.wait("gpsimd", "c_cc2")
        dma("gpsimd", hmrow2[:], cm2_o.ap(), "qcc")
        qcc_hm = P.now("qcc")

        P.wait("sync", "c_h")
        P.wait("sync", "qst", wo_spill[7])
        for k in range(NK):
            dma("sync", wob_ch(k), wotb_sp[k * 128:(k + 1) * 128, :], "qtp")
        s_spill_m = {}
        for th in range(2):
            P.wait("sync", "qst", nrm_spill_all)
            if th == 1:
                P.wait("sync", "c_wo", 8)
            for k in range(NK):
                dma("sync", nrmf_tile(k),
                    nrm_sp[th * 1024:(th + 1) * 1024, k * 128:(k + 1) * 128], "qtp", transpose=True)
            qtp_n = P.now("qtp")
            for m in range(8):
                mc = th * 8 + m
                P.wait("tensor", "qtp", qtp_n)
                if mc >= 1:
                    P.wait("tensor", "c_sev", mc)
                for n in range(2):
                    for k in range(NK):
                        P.emit("tensor", lambda t, kk=k, nn=n, mm=m:
                               t.matmul(PB[:, nn * 512:(nn + 1) * 512],
                                        nrmf_tile(kk)[:, mm * 128:(mm + 1) * 128],
                                        wob_ch(kk)[:, nn * 512:(nn + 1) * 512],
                                        start=(kk == 0), stop=False))
                    P.emit("tensor", lambda t, nn=n:
                           t.matmul(PB[:, nn * 512:(nn + 1) * 512], ones1[:],
                                    bob[0:1, nn * 512:(nn + 1) * 512], start=False, stop=True))
                wom = P.inc_prev("tensor", "c_wo")
                if mc >= 2:
                    P.wait("sync", "c_sev", mc - 1)
                dma("sync", xnr[:, (mc % 2) * D:(mc % 2 + 1) * D],
                    xn_sp[mc * 128:(mc + 1) * 128, :], "qx2")
                qm = P.now("qx2")
                P.wait("vector", "c_wo", wom)
                P.wait("vector", "qx2", qm)
                ssl2 = sbufs[:, (mc % 2) * D:(mc % 2 + 1) * D]
                if mc >= 2:
                    P.wait("vector", "qst", s_spill_m[mc - 2])
                sev = P.inc("vector", lambda v, o=ssl2, b=xnr[:, (mc % 2) * D:(mc % 2 + 1) * D]:
                            v.tensor_add(o, PB[:, :], b), "c_sev")
                P.wait("scalar", "c_sev", sev)
                s_spill_m[mc] = dma("scalar", s_sp[mc * 128:(mc + 1) * 128, :], ssl2, "qst")

        P.wait("vector", "qcc", qcc_hm)
        P.emit("vector", lambda v: v.tensor_mul(sc[0:1, 9:10], sc[0:1, 1:2], hmrow2[0:1, 3:4]))
        vgap()
        tshm = P.inc("vector", lambda v: v.tensor_scalar(sc[0:1, 6:7], sc[0:1, 9:10],
                                                         float(1.0 / 127.0), None, ALU.mult), "c_rsh")
        P.wait("scalar", "c_rsh", tshm)
        P.emit("scalar", lambda s: s.activation(sc[0:1, 7:8], sc[0:1, 9:10], AF.Ln,
                                                scale=f32(1.0 / 127.0)))
        ssync()
        rshm = P.inc("scalar", lambda s: s.activation(sc[0:1, 7:8], sc[0:1, 7:8], AF.Exp,
                                                      scale=f32(-1.0)), "c_rsh")
        P.wait("tensor", "c_rsh", rshm)
        P.wait("tensor", "c_sev")
        P.inc("tensor", lambda t: t.matmul(PC[:, 0:2], onesft[:], sc[0:1, 6:8],
                                           start=True, stop=True), "c_pbc")
        P.wait("vector", "c_pbc", P.now("c_pbc"))
        rshc = P.inc("vector", lambda v: v.tensor_copy(sccol[:, 2:4], PC[:, 0:2]), "c_rsh")
        P.wait("scalar", "c_gq")
        for cc_ in range(NK):
            dma("scalar", swd_sp[0:1, cc_ * 128:(cc_ + 1) * 128].rearrange("a b -> b a"),
                swdc[:, cc_:cc_ + 1], "qmx")
        P.wait("sync", "qmx")
        dma("sync", swdrow, swd_sp, "qs2r")
        P.wait("vector", "qs2r", P.now("qs2r"))
        P.wait("vector", "c_rsh", rshc)
        vgap()
        shm = P.inc("vector", lambda v: v.tensor_scalar(shrow, onesft[0:1, 0:128], sc[0:1, 6:7],
                                                        None, ALU.mult), "c_swd")
        P.wait("tensor", "c_swd", shm)
        P.wait("tensor", "c_rsh", rshc)
        for n in range(2):
            P.emit("tensor", lambda t, nn=n:
                   t.matmul(PC[:, nn * 512:(nn + 1) * 512], shrow,
                            swdrow[0:1, nn * 512:(nn + 1) * 512], start=True, stop=True))
        swm = P.inc_prev("tensor", "c_swd")
        P.wait("vector", "c_swd", swm)
        swe = P.inc("vector", lambda v: v.tensor_copy(swdb, PC[:, :]), "c_swd")

        ihf_done = {}
        sm_done = {}
        jstep = 0
        for mi in range(NI):
            for half in range(2):
                if jstep == 0:
                    P.wait("sync", "qh", hsp_m[43])
                if jstep >= 2:
                    P.wait("sync", "c_ihf", sm_done[jstep - 2])
                dma("sync", hin_sl(jstep),
                    h_sp[mi * 128:(mi + 1) * 128, half * 1024:(half + 1) * 1024], "qh")
                qm = P.now("qh")
                fsl = fsc_sl(jstep)
                P.wait("scalar", "qh", qm)
                if jstep == 0:
                    P.wait("scalar", "c_rsh", rshc)
                if jstep >= 2:
                    P.wait("scalar", "c_ihv", ihf_done[jstep - 2])
                P.emit("scalar", lambda s, a=fsl, b=hin_sl(jstep):
                       s.activation(a, b, AF.Identity, scale=sccol[:, 3:4], bias=MAGIC))
                sm = sm_done[jstep] = P.inc_prev("scalar", "c_ihf")
                P.wait("vector", "c_ihf", sm)
                P.emit("vector", lambda v, a=fsl: v.tensor_scalar(a, a, -12582912.0, 127.0, ALU.add, ALU.min))
                vgap()
                ihf_done[jstep] = P.inc(
                    "vector", lambda v, o=ihf_tile(mi)[:, half * 1024:(half + 1) * 1024], a=fsl:
                    v.tensor_scalar(o, a, -128.0, None, ALU.max), "c_ihv")
                jstep += 1
        ihf_all = P.now("c_ihv")

        P.wait("sync", "qst", qst_after_wq)
        P.wait("sync", "c_wo", 16)
        P.wait("sync", "c_sev", 16)
        P.wait("sync", "c_nrm", 16)
        P.wait("sync", "qst", s_spill_m[NCH - 1])
        for kk in range(NI):
            dma("sync", iwdT_tile(kk), iwd_sp[0:D, kk * 128:(kk + 1) * 128], "qtp", transpose=True)
        qtp_d = P.now("qtp")

        out_m = {}
        for mc in range(NCH):
            P.wait("tensor", "qtp", qtp_d)
            if mc == 0:
                P.wait("tensor", "c_ihv", ihf_all)
                P.wait("tensor", "c_ihf", P.now("c_ihf"))
            if mc >= 1:
                P.wait("tensor", "c_dt", mc)
            for n in range(2):
                for kk in range(NI):
                    P.emit("tensor", lambda t, k2=kk, nn=n, mm=mc:
                           t.matmul(PA[:, nn * 512:(nn + 1) * 512],
                                    ihf_tile(k2)[:, mm * 128:(mm + 1) * 128],
                                    iwdT_tile(k2)[:, nn * 512:(nn + 1) * 512],
                                    start=(k2 == 0), stop=(k2 == NI - 1)))
            dnm = P.inc_prev("tensor", "c_dn")
            if mc >= 2:
                P.wait("sync", "c_fin", mc - 1)
            dma("sync", sbufs[:, (mc % 2) * D:(mc % 2 + 1) * D],
                s_sp[mc * 128:(mc + 1) * 128, :], "qs2r")
            dma("sync", xr2[:, (mc % 2) * D:(mc % 2 + 1) * D],
                x_in[mc * 128:(mc + 1) * 128, :], "qx2")
            qm1 = P.now("qs2r"); qm2 = P.now("qx2")
            P.wait("vector", "c_dn", dnm)
            P.wait("vector", "c_swd", swe)
            fsl = fsc_sl(mc)
            if mc >= 2:
                P.wait("vector", "qst", out_m[mc - 2])
            dtm = P.inc("vector", lambda v, o=fsl: v.tensor_mul(o, PA[:, :], swdb), "c_dt")
            P.wait("scalar", "c_dt", dtm)
            gsl = PB[:, 0:1024]
            dgm = P.inc("scalar", lambda s, o=gsl, a=fsl: s.activation(o, a, AF.Silu), "c_dg")
            P.wait("vector", "c_dg", dgm)
            P.wait("vector", "qs2r", qm1)
            P.wait("vector", "qx2", qm2)
            P.emit("vector", lambda v, o=fsl, g2=gsl, s2=sbufs[:, (mc % 2) * D:(mc % 2 + 1) * D]:
                   v.tensor_mul(o, g2, s2))
            vgap()
            fin = P.inc("vector", lambda v, o=fsl, b=xr2[:, (mc % 2) * D:(mc % 2 + 1) * D]:
                        v.tensor_add(o, o, b), "c_fin")
            P.wait("scalar", "c_fin", fin)
            out_m[mc] = dma("scalar", out_hbm[mc * 128:(mc + 1) * 128, :], fsl, "qst")

        if debug or debug2:
            P.wait("scalar", "c_fin", P.now("c_fin"))
            dma("scalar", swdcdbg[:, 0:8], swdc[:, 0:8], "qst")
            dma("scalar", swdbdbg, swdb, "qst")
        if debug:
            P.wait("scalar", "c_rsh", rshc)
            dma("scalar", scdbg, sc[:], "qst")
            P.wait("scalar", "c_fin", P.now("c_fin"))
            for di, dt_ in enumerate([sumx, sumq, mcol, tcol, vcol, icol]):
                dma("scalar", stdbg[:, di * 16:(di + 1) * 16], dt_[:, 0:16], "qst")

        for s_ in ["qst", "qh", "qmx", "qcc", "qtp", "qxn", "qx2", "qs2r", "qld"]:
            P.wait("sync", s_, P.now(s_))

        assert not P.bundles and P.cur_bundle is None
        block = es.enter_context(nc.Block())

        def replay(eng_name):
            def body(e):
                for item in P.streams[eng_name]:
                    if item[0] == "op":
                        item[1](e)
                    elif item[0] == "inc":
                        item[1](e).then_inc(sems[item[2]], item[3])
                    else:
                        e.wait_ge(sems[item[1]], item[2])
            return body

        block.sync(replay("sync"))
        block.scalar(replay("scalar"))
        block.vector(replay("vector"))
        block.tensor(replay("tensor"))
        block.gpsimd(replay("gpsimd"))

    return nc


_PERM = np.array([4 * (n % 256) + n // 256 for n in range(D)], dtype=np.int64)


def _consts():
    ltt = np.triu(np.ones((128, 128))).astype(BF)
    oneh = np.zeros((128, NCH * 17), dtype=BF)
    for c in range(NCH):
        oneh[:, c * 17 + 0] = 1.0
        oneh[:, c * 17 + c + 1] = 1.0
    slt = np.zeros((17, 16), dtype=np.float32)
    slt[0, :] = 1.0
    for m in range(16):
        for j in range(m):
            slt[1 + j, m] = 1.0
    ones1b = np.ones((1, 128), dtype=BF)
    onesf = np.ones((1, 128), dtype=np.float32)
    ohr = np.zeros((16, NCH * 128), dtype=BF)
    for mc in range(NCH):
        ohr[mc, mc * 128:(mc + 1) * 128] = 1.0
    return ltt, oneh, slt, ones1b, onesf, ohr


def kernel(L0_attn_x, L0_attn_Wk, L0_attn_bk, L0_attn_Wv, L0_attn_bv,
           L0_attn_ln_g, L0_attn_ln_b, L0_attn_Wo, L0_attn_bo,
           L0_ffn_clip_x, L0_ffn_clip_down, L0_ffn_w_gate, L0_ffn_w_up,
           L0_ffn_w_down, _debug=False, _debug2=False):
    x = np.asarray(L0_attn_x, dtype=np.float32)
    Wk = np.asarray(L0_attn_Wk, dtype=np.float32)
    bk = np.asarray(L0_attn_bk, dtype=np.float32)
    Wv = np.asarray(L0_attn_Wv, dtype=np.float32)
    bv = np.asarray(L0_attn_bv, dtype=np.float32)
    ln_g = np.asarray(L0_attn_ln_g, dtype=np.float32)
    ln_b = np.asarray(L0_attn_ln_b, dtype=np.float32)
    Wo = np.asarray(L0_attn_Wo, dtype=np.float32)
    bo = np.asarray(L0_attn_bo, dtype=np.float32)
    clip_x = np.asarray(L0_ffn_clip_x, dtype=np.float32).reshape(1, 1)
    clip_d = np.asarray(L0_ffn_clip_down, dtype=np.float32).reshape(1, 1)
    wg = np.asarray(L0_ffn_w_gate, dtype=np.float32)
    wu = np.asarray(L0_ffn_w_up, dtype=np.float32)
    wd = np.asarray(L0_ffn_w_down, dtype=np.float32)

    p = _PERM
    xp = np.ascontiguousarray(x[:, :, p])
    wkTh = np.ascontiguousarray(Wk[p][:, p].T)
    wvTh = np.ascontiguousarray(Wv[p][:, p].T)
    woTh = np.ascontiguousarray(Wo[p][:, p].T)
    bkrh = np.ascontiguousarray(bk[p].reshape(1, D))
    bvrh = np.ascontiguousarray(bv[p].reshape(1, D))
    borh = np.ascontiguousarray(bo[p].reshape(1, D))
    lngh = np.ascontiguousarray(ln_g[p].reshape(NK, 128).T)
    lnbh = np.ascontiguousarray(ln_b[p].reshape(NK, 128).T)
    wgp = np.ascontiguousarray(wg[:, p])
    wup = np.ascontiguousarray(wu[:, p])
    wdp = np.ascontiguousarray(wd[p, :])

    ltt, oneh, slt, ones1b, onesf, ohr = _consts()

    in_maps = []
    for core in range(8):
        b, h = core // 2, core % 2
        pos = (h * T + np.arange(NCH)[None, :] * 128 + np.arange(128)[:, None] + 1)
        epspos = (1e-5 * pos).astype(np.float32)
        in_maps.append(dict(
            x=np.ascontiguousarray(xp[b, h * T:(h + 1) * T, :]),
            wkT=wkTh, wvT=wvTh, woT=woTh, bkr=bkrh, bvr=bvrh, bor=borh,
            lng=lngh, lnb=lnbh, wg=wgp, wu=wup, wd=wdp,
            clipx=clip_x, clipd=clip_d,
            ltt=ltt, oneh=oneh, slt=slt, ones1b=ones1b, onesf=onesf, ohr=ohr,
            epspos=epspos, ccmask=np.full((1, 1), float(h), dtype=np.float32),
        ))

    nc = build_nc(debug=_debug, debug2=_debug2)
    res = run_bass_kernel_spmd(nc, in_maps, core_ids=list(range(8)))

    out = np.empty((4, 4096, D), dtype=np.float32)
    for core in range(8):
        b, h = core // 2, core % 2
        op = np.asarray(res.results[core]["out"])
        out[b, h * T:(h + 1) * T, :][:, p] = op
    if _debug or _debug2:
        return out, res
    return out
